# version 05
# baseline (speedup 1.0000x reference)
import numpy as np
from contextlib import ExitStack
import concourse.bass as bass
import concourse.mybir as mybir
from concourse.bass_utils import run_bass_kernel_spmd

F32 = mybir.dt.float32
BF16 = mybir.dt.bfloat16
AF = mybir.ActivationFunctionType
ALU = mybir.AluOpType

D, NH, DH, NCD, DHID, NCH = 1024, 16, 64, 8, 4096, 32
NT = 512
EPS_LN = 1e-5
EPS_ATT = 1e-6
ATT_SCALE = DH ** -0.5

ENGS = ("pe", "act", "dve", "sy", "gp")


class Builder:
    def __init__(self, S, stage="full"):
        self.S = S
        self.T = S // 4
        self.TSUP = min(2048, S)
        self.QM = min(1024, self.T)
        self.TH = min(1024, self.T)
        self.stage = stage
        self.nc = bass.Bass("TRN2", target_bir_lowering=False)
        self.ctx = ExitStack()
        self.streams = {e: [] for e in ENGS}
        self.sems = {}
        self.cnt = {}
        self.bar_n = 0
        self._uid = 0

    def sem(self, name):
        h = self.ctx.enter_context(self.nc.semaphore(name))
        self.sems[name] = h
        self.cnt[name] = 0
        return name

    def emit(self, eng, fn, waits=(), incs=()):
        for s, n in incs:
            self.cnt[s] += n
        wl = [(self.sems[s], v) for s, v in waits]
        il = [(self.sems[s], n) for s, n in incs]

        def closure(e, fn=fn, wl=wl, il=il):
            for h, v in wl:
                e.wait_ge(h, v)
            inst = fn(e) if fn is not None else None
            for h, n in il:
                assert inst is not None
                inst.then_inc(h, n)
        self.streams[eng].append(closure)

    def c(self, s):
        return self.cnt[s]

    def u(self, base):
        self._uid += 1
        return f"{base}_{self._uid}"

    def warmup(self, n=24, extra_waits=()):
        with self.nc.psum_tensor(self.u("warm"), [128, NT], F32) as wp:
            def mk(e, wp=wp, n=n):
                for _ in range(n):
                    inst = e.matmul(wp[:], self.warmsrc[:, 0:128],
                                    self.warmsrc[:, 128:640], start=True, stop=True)
                return inst
            self.emit("pe", mk, waits=[("gp", 1)] + list(extra_waits) + self.guard(),
                      incs=[("pe", 1)])

    def guard(self):
        g = []
        if self.cnt["dve"]:
            g.append(("dve", self.cnt["dve"]))
        if self.cnt["act"]:
            g.append(("act", self.cnt["act"]))
        return g

    def barrier(self):
        self.bar_n += 1
        k = self.bar_n
        waits = [(s, self.cnt[s]) for s in self.cnt if s != "bar" and self.cnt[s] > 0]
        self.emit("sy", lambda e: e.sem_inc(self.sems["bar"], 1), waits=waits)
        self.cnt["bar"] += 1
        for eng in ("pe", "act", "dve", "gp"):
            self.emit(eng, None, waits=[("bar", k)])

    def build(self):
        nc, S, T = self.nc, self.S, self.T
        ctx = self.ctx

        self.xT = nc.dram_tensor("xT", [D, S], F32, kind="ExternalInput").ap()
        self.wq = nc.dram_tensor("wq", [128, NCD * D], F32, kind="ExternalInput").ap()
        self.wk = nc.dram_tensor("wk", [128, NCD * D], F32, kind="ExternalInput").ap()
        self.wv = nc.dram_tensor("wv", [128, NCD * D], F32, kind="ExternalInput").ap()
        self.wo = nc.dram_tensor("wo", [128, NCD * D], F32, kind="ExternalInput").ap()
        self.w1 = nc.dram_tensor("w1", [NCH, 128, NCD * 128], F32, kind="ExternalInput").ap()
        self.w2 = nc.dram_tensor("w2", [NCH, 128, NCD * 128], F32, kind="ExternalInput").ap()
        self.h1 = nc.dram_tensor("h1", [NCH, 128, NCH * 128], F32, kind="ExternalInput").ap()
        self.h2 = nc.dram_tensor("h2", [NCH, 128, NCH * 128], F32, kind="ExternalInput").ap()
        self.w3 = nc.dram_tensor("w3", [NCD, 128, NCH * 128], F32, kind="ExternalInput").ap()
        self.s1 = nc.dram_tensor("s1", [128, NCH], F32, kind="ExternalInput").ap()
        self.s2 = nc.dram_tensor("s2", [128, NCH], F32, kind="ExternalInput").ap()
        self.hs1 = nc.dram_tensor("hs1", [128, NCH], F32, kind="ExternalInput").ap()
        self.hs2 = nc.dram_tensor("hs2", [128, NCH], F32, kind="ExternalInput").ap()
        self.s3 = nc.dram_tensor("s3", [128, NCD], F32, kind="ExternalInput").ap()
        self.Esel = nc.dram_tensor("Esel", [16, D], F32, kind="ExternalInput").ap()
        self.out = nc.dram_tensor("out", [D, T], F32, kind="ExternalOutput").ap()

        self.K_sc = nc.dram_tensor("K_sc", [NCD, 128, S], BF16).ap()
        self.V_sc = nc.dram_tensor("V_sc", [NH, 128, S // 128, DH + 1], BF16).ap()
        self.y1_sc = nc.dram_tensor("y1_sc", [NCD, 128, T], F32).ap()
        self.RS_sc = nc.dram_tensor("RS_sc", [16, T], F32).ap()

        for s in ["bar", "pe", "act", "dve", "gp",
                  "ld_x0", "ld_x1",
                  "ld_w0", "ld_w1", "ld_w2",
                  "ld_k0", "ld_k1",
                  "ld_v0", "ld_v1",
                  "ld_y0", "ld_y1",
                  "ld_c",
                  "st_k0", "st_k1",
                  "st_v0", "st_v1",
                  "st_y0", "st_y1",
                  "st_rs",
                  "st_o0", "st_o1"]:
            self.sem(s)

        self.onesF = ctx.enter_context(nc.sbuf_tensor("onesF", [128, 1], F32))
        self.onesB = ctx.enter_context(nc.sbuf_tensor("onesB", [128, 1], BF16))
        self.onesR = ctx.enter_context(nc.sbuf_tensor("onesR", [1, 128], F32))
        self.s1b = ctx.enter_context(nc.sbuf_tensor("s1b", [128, NCH], F32))
        self.s2b = ctx.enter_context(nc.sbuf_tensor("s2b", [128, NCH], F32))
        self.hs1b = ctx.enter_context(nc.sbuf_tensor("hs1b", [128, NCH], F32))
        self.hs2b = ctx.enter_context(nc.sbuf_tensor("hs2b", [128, NCH], F32))
        self.s3b = ctx.enter_context(nc.sbuf_tensor("s3b", [128, NCD], F32))

        self.warmsrc = ctx.enter_context(nc.sbuf_tensor("warmsrc", [128, 640], BF16))
        self.emit("gp", lambda e: e.memset(self.warmsrc[:], 1.0), incs=[("gp", 1)])
        self.emit("gp", lambda e: e.memset(self.onesF[:], 1.0), incs=[("gp", 1)])
        self.emit("gp", lambda e: e.memset(self.onesB[:], 1.0), incs=[("gp", 1)])
        self.emit("gp", lambda e: e.memset(self.onesR[:], 1.0), incs=[("gp", 1)])
        for src, dst in [(self.s1, self.s1b), (self.s2, self.s2b),
                         (self.hs1, self.hs1b), (self.hs2, self.hs2b), (self.s3, self.s3b)]:
            self.emit("sy", lambda e, s=src, d=dst: e.dma_start(d[:], s), incs=[("ld_c", 16)])

        scope_q = ExitStack()
        scope_o = ExitStack()
        self.qbuf = scope_q.enter_context(nc.sbuf_tensor("qbuf", [128, NCD * T], BF16))
        self.phase0()
        self.barrier()
        if self.stage == "p0":
            self.debug_p0()
            self.finish()
            scope_q.close()
            scope_o.close()
            return
        self.obuf = scope_o.enter_context(nc.sbuf_tensor("obuf", [128, NCD * T], BF16, side="right"))
        self.phase1()
        self.barrier()
        scope_q.close()
        if self.stage == "p1":
            self.debug_p1()
            self.finish()
            scope_o.close()
            return
        scope_r = ExitStack()
        self.r2row = scope_r.enter_context(nc.sbuf_tensor("r2row", [1, T], F32))
        self.mr2row = scope_r.enter_context(nc.sbuf_tensor("mr2row", [1, T], F32))
        self.phase2()
        self.barrier()
        scope_o.close()
        if self.stage == "p2":
            self.debug_p2()
            self.finish()
            scope_r.close()
            return
        self.phase3()
        scope_r.close()
        self.finish()

    def phase0(self):
        nc, S, T, TSUP = self.nc, self.S, self.T, self.TSUP
        NSUP = S // TSUP
        NTT = TSUP // NT

        with ExitStack() as sc:
            wqb = sc.enter_context(nc.sbuf_tensor("wqb", [128, NCD * D], BF16))
            wkb = sc.enter_context(nc.sbuf_tensor("wkb", [128, NCD * D], BF16))
            wvb = sc.enter_context(nc.sbuf_tensor("wvb", [128, NCD * D], BF16))
            xs = [sc.enter_context(nc.sbuf_tensor(f"xs{i}", [128, TSUP], F32)) for i in range(2)]
            sq = [sc.enter_context(nc.sbuf_tensor(f"sq{i}", [128, TSUP], BF16)) for i in range(2)]
            lnb = sc.enter_context(nc.sbuf_tensor("lnb", [128, NCD * TSUP], BF16))
            kvo = [sc.enter_context(nc.sbuf_tensor(f"kvo{i}", [128, TSUP], BF16)) for i in range(2)]
            vvo = [sc.enter_context(nc.sbuf_tensor(f"vvo{i}", [128, NH * (DH + 1)], BF16))
                   for i in range(2)]
            self.alloc_rows(sc, TSUP, "p0")
            for b in range(2):
                for hh in range(NH):
                    self.emit("gp", lambda e, b=b, hh=hh: e.memset(
                        vvo[b][:, hh * (DH + 1) + DH: hh * (DH + 1) + DH + 1], 1.0),
                        incs=[("gp", 1)])

            self.warmup()
            self.emit("gp", lambda e: e.dma_start(wqb[:], self.wq), incs=[("ld_w0", 16)])
            self.emit("gp", lambda e: e.dma_start(wkb[:], self.wk), incs=[("ld_w0", 16)])
            self.emit("gp", lambda e: e.dma_start(wvb[:], self.wv), incs=[("ld_w0", 16)])
            wproj_ready = self.c("ld_w0")

            xs_free = [[], []]
            sq_free = [[], []]
            kvo_free = [[], []]
            vvo_free = [[], []]

            for sup in range(NSUP):
                ts0 = sup * TSUP
                with ExitStack() as pctx:
                    ps_sum = pctx.enter_context(nc.psum_tensor(self.u("ps_sum"), [1, TSUP], F32))
                    ps_ssq = pctx.enter_context(nc.psum_tensor(self.u("ps_ssq"), [1, TSUP], F32))
                    for ic in range(NCD):
                        par = ic % 2
                        self.emit("sy",
                                  lambda e, p=par, i=ic, ts0=ts0: e.dma_start(
                                      xs[p][:], self.xT[i * 128:(i + 1) * 128, ts0:ts0 + TSUP]),
                                  waits=xs_free[par], incs=[(f"ld_x{par}", 16)])
                        ldv = self.c(f"ld_x{par}")
                        self.emit("act",
                                  lambda e, p=par: e.activation(sq[p][:], xs[p][:], AF.Square),
                                  waits=[(f"ld_x{par}", ldv)] + sq_free[par], incs=[("act", 1)])
                        actv = self.c("act")

                        def mk_stats(e, p=par, i=ic, ps_sum=ps_sum, ps_ssq=ps_ssq):
                            for tt in range(NTT):
                                e.matmul(ps_sum[0:1, tt * NT:(tt + 1) * NT], self.onesF[:],
                                         xs[p][:, tt * NT:(tt + 1) * NT],
                                         start=(i == 0), stop=(i == NCD - 1))
                            for tt in range(NTT):
                                inst = e.matmul(ps_ssq[0:1, tt * NT:(tt + 1) * NT], self.onesB[:],
                                                sq[p][:, tt * NT:(tt + 1) * NT],
                                                start=(i == 0), stop=(i == NCD - 1))
                            return inst
                        gw = self.guard() if ic == 0 else []
                        self.emit("pe", mk_stats,
                                  waits=[(f"ld_x{par}", ldv), ("act", actv), ("gp", 4)] + gw,
                                  incs=[("pe", 1)])
                        xs_free[par] = [("pe", self.c("pe"))]
                        sq_free[par] = [("pe", self.c("pe"))]
                    rows_done = self.ln_rows(ps_sum, ps_ssq, TSUP, self.rows["r"], self.rows["mr"])

                with ExitStack() as pctx:
                    psR = pctx.enter_context(nc.psum_tensor(self.u("psR"), [128, TSUP], F32))
                    psMR = pctx.enter_context(nc.psum_tensor(self.u("psMR"), [128, TSUP], F32))
                    bc_done = self.bcast(psR, psMR, self.rows["r"], self.rows["mr"], 0, NTT,
                                         waits=[("dve", rows_done)])
                    for ic in range(NCD):
                        par = ic % 2
                        self.emit("sy",
                                  lambda e, p=par, i=ic, ts0=ts0: e.dma_start(
                                      xs[p][:], self.xT[i * 128:(i + 1) * 128, ts0:ts0 + TSUP]),
                                  waits=xs_free[par], incs=[(f"ld_x{par}", 16)])
                        ldv = self.c(f"ld_x{par}")
                        self.emit("dve", lambda e, p=par, psR=psR: e.tensor_mul(
                            xs[p][:], xs[p][:], psR[:]),
                            waits=[(f"ld_x{par}", ldv), ("pe", bc_done)], incs=[("dve", 1)])
                        self.emit("dve", lambda e, p=par, i=ic, psMR=psMR: e.tensor_sub(
                            lnb[:, i * TSUP:(i + 1) * TSUP], xs[p][:], psMR[:]),
                            waits=[("dve", self.c("dve"))], incs=[("dve", 1)])
                        xs_free[par] = [("dve", self.c("dve"))]
                    ln_ready = self.c("dve")

                with ExitStack() as pctx:
                    psP = [pctx.enter_context(nc.psum_tensor(self.u(f"psP{i}"), [128, TSUP], F32))
                           for i in range(2)]
                    psP_free = [[], []]
                    for oc in range(NCD):
                        par = oc % 2

                        def mk_kproj(e, o=oc, p=par, psP=psP):
                            for ic in range(NCD):
                                for tt in range(NTT):
                                    inst = e.matmul(
                                        psP[p][:, tt * NT:(tt + 1) * NT],
                                        wkb[:, ic * D + o * 128: ic * D + o * 128 + 128],
                                        lnb[:, ic * TSUP + tt * NT: ic * TSUP + (tt + 1) * NT],
                                        start=(ic == 0), stop=(ic == NCD - 1))
                            return inst
                        self.emit("pe", mk_kproj,
                                  waits=[("dve", ln_ready), ("ld_w0", wproj_ready)] + psP_free[par],
                                  incs=[("pe", 1)])
                        mmv = self.c("pe")
                        self.emit("dve", lambda e, p=par, psP=psP: e.tensor_copy(kvo[p][:], psP[p][:]),
                                  waits=[("pe", mmv)] + kvo_free[par], incs=[("dve", 1)])
                        psP_free[par] = [("dve", self.c("dve"))]
                        dvv = self.c("dve")
                        self.emit("sy", lambda e, p=par, o=oc, ts0=ts0: e.dma_start(
                            self.K_sc[o, :, ts0:ts0 + TSUP], kvo[p][:]),
                            waits=[("dve", dvv)], incs=[(f"st_k{par}", 16)])
                        kvo_free[par] = [(f"st_k{par}", self.c(f"st_k{par}"))]
                    if sup == 0:
                        NTTQ = T // NT
                        for oc in range(NCD):
                            par = oc % 2

                            def mk_qproj(e, o=oc, p=par, psP=psP):
                                for ic in range(NCD):
                                    for tt in range(NTTQ):
                                        inst = e.matmul(
                                            psP[p][:, tt * NT:(tt + 1) * NT],
                                            wqb[:, ic * D + o * 128: ic * D + o * 128 + 128],
                                            lnb[:, ic * TSUP + tt * NT: ic * TSUP + (tt + 1) * NT],
                                            start=(ic == 0), stop=(ic == NCD - 1))
                                return inst
                            self.emit("pe", mk_qproj,
                                      waits=[("dve", ln_ready), ("ld_w0", wproj_ready)] + psP_free[par],
                                      incs=[("pe", 1)])
                            mmv = self.c("pe")
                            self.emit("dve", lambda e, p=par, o=oc, psP=psP: e.tensor_copy(
                                self.qbuf[:, o * T:(o + 1) * T], psP[p][:, 0:T]),
                                waits=[("pe", mmv)], incs=[("dve", 1)])
                            psP_free[par] = [("dve", self.c("dve"))]

                with ExitStack() as pctx:
                    psV = [pctx.enter_context(nc.psum_tensor(self.u(f"psV{i}"), [128, D], F32))
                           for i in range(2)]
                    psV_free = [self.guard(), self.guard()]
                    for tk in range(TSUP // 128):
                        par = tk % 2

                        def mk_vproj(e, t=tk, p=par, psV=psV):
                            for ic in range(NCD):
                                for oh in range(2):
                                    inst = e.matmul(
                                        psV[p][:, oh * NT:(oh + 1) * NT],
                                        lnb[:, ic * TSUP + t * 128: ic * TSUP + t * 128 + 128],
                                        wvb[:, ic * D + oh * NT: ic * D + (oh + 1) * NT],
                                        start=(ic == 0), stop=(ic == NCD - 1))
                            return inst
                        self.emit("pe", mk_vproj,
                                  waits=[("dve", ln_ready), ("ld_w0", wproj_ready)] + psV_free[par],
                                  incs=[("pe", 1)])
                        mmv = self.c("pe")
                        def mk_vevac(e, p=par, psV=psV):
                            dst = bass.AP(vvo[p], 0, [[NH * (DH + 1), 128], [DH + 1, NH], [1, DH]])
                            return e.tensor_copy(dst, psV[p][:])
                        self.emit("dve", mk_vevac,
                                  waits=[("pe", mmv), ("gp", self.c("gp"))] + vvo_free[par],
                                  incs=[("dve", 1)])
                        psV_free[par] = [("dve", self.c("dve"))]
                        dvv = self.c("dve")
                        tkg = (ts0 + tk * 128) // 128
                        NKC_ALL = self.S // 128

                        def mk_vst(e, p=par, tkg=tkg, NKC_ALL=NKC_ALL):
                            dst = bass.AP(self.V_sc.tensor, tkg * (DH + 1),
                                          [[NKC_ALL * (DH + 1), 128],
                                           [128 * NKC_ALL * (DH + 1), NH], [1, DH + 1]])
                            src = bass.AP(vvo[p], 0,
                                          [[NH * (DH + 1), 128], [DH + 1, NH], [1, DH + 1]])
                            return e.dma_start(dst, src)
                        self.emit("sy", mk_vst, waits=[("dve", dvv)], incs=[(f"st_v{par}", 16)])
                        vvo_free[par] = [(f"st_v{par}", self.c(f"st_v{par}"))]

    def alloc_rows(self, scope, W, tag):
        nc = self.nc
        r = {}
        for n in ("m", "v", "t", "r", "mr"):
            r[n] = scope.enter_context(nc.sbuf_tensor(f"row_{tag}_{n}", [1, W], F32))
        self.rows = r

    def ln_rows(self, ps_sum, ps_ssq, W, rdst, mrdst, roff=0):
        pev = self.c("pe")
        ro = self.rows
        self.emit("dve", lambda e, ro=ro: e.tensor_scalar_mul(
            ro["m"][:, 0:W], ps_sum[:], 1.0 / D),
            waits=[("pe", pev)], incs=[("dve", 1)])
        self.emit("dve", lambda e, ro=ro: e.tensor_scalar(
            ro["v"][:, 0:W], ps_ssq[:], 1.0 / D, EPS_LN, ALU.mult, ALU.add),
            waits=[("pe", pev)], incs=[("dve", 1)])
        d0 = self.c("dve")
        self.emit("dve", lambda e, ro=ro: e.tensor_mul(
            ro["t"][:, 0:W], ro["m"][:, 0:W], ro["m"][:, 0:W]),
            waits=[("dve", d0)], incs=[("dve", 1)])
        self.emit("dve", lambda e, ro=ro: e.tensor_sub(
            ro["v"][:, 0:W], ro["v"][:, 0:W], ro["t"][:, 0:W]),
            waits=[("dve", self.c("dve"))], incs=[("dve", 1)])
        dv = self.c("dve")
        self.emit("act", lambda e, ro=ro: e.activation(
            ro["t"][:, 0:W], ro["v"][:, 0:W], AF.Sqrt),
            waits=[("dve", dv)], incs=[("act", 1)])
        av = self.c("act")
        self.emit("dve", lambda e, ro=ro: e.reciprocal(
            rdst[:, roff:roff + W], ro["t"][:, 0:W]),
            waits=[("act", av)], incs=[("dve", 1)])
        self.emit("dve", lambda e, ro=ro: e.tensor_mul(
            mrdst[:, roff:roff + W], ro["m"][:, 0:W], rdst[:, roff:roff + W]),
            waits=[("dve", self.c("dve"))], incs=[("dve", 1)])
        return self.c("dve")

    def bcast(self, psR, psMR, rsrc, mrsrc, roff, ntt, waits):
        def mk(e):
            for tt in range(ntt):
                e.matmul(psR[:, tt * NT:(tt + 1) * NT], self.onesR[:],
                         rsrc[0:1, roff + tt * NT: roff + (tt + 1) * NT], start=True, stop=True)
            for tt in range(ntt):
                inst = e.matmul(psMR[:, tt * NT:(tt + 1) * NT], self.onesR[:],
                                mrsrc[0:1, roff + tt * NT: roff + (tt + 1) * NT], start=True, stop=True)
            return inst
        self.emit("pe", mk, waits=list(waits) + self.guard(), incs=[("pe", 1)])
        return self.c("pe")

    def phase1(self):
        nc, S, T = self.nc, self.S, self.T
        QM = 512
        NKC = S // 128
        NQM = T // QM
        KCT = 3
        tiles = []
        k = 0
        while k < NKC:
            n = min(KCT, NKC - k)
            tiles.append((k, n))
            k += n
        NKC_ALL = NKC

        with ExitStack() as sc:
            kbuf = [sc.enter_context(nc.sbuf_tensor(f"kbuf{i}", [128, S], BF16)) for i in range(2)]
            vbuf = [sc.enter_context(nc.sbuf_tensor(f"vbuf{i}", [128, 8 * (DH + 1)], BF16))
                    for i in range(2)]
            wtil = [sc.enter_context(nc.sbuf_tensor(f"wtil{i}", [128, KCT * QM], BF16))
                    for i in range(2)]
            rs_row = [sc.enter_context(nc.sbuf_tensor(f"rs_row{i}", [1, QM], F32)) for i in range(2)]
            dummy = sc.enter_context(nc.sbuf_tensor("sig_pre", [1, NT], BF16))
            with ExitStack() as pctx:
                psS = [pctx.enter_context(nc.psum_tensor(f"psS{i}", [128, KCT * QM], F32))
                       for i in range(2)]
                psO = [pctx.enter_context(nc.psum_tensor(f"psO{i}", [DH + 1, QM], F32))
                       for i in range(2)]

                kbuf_free = [[], []]
                rs_free = [[], []]
                vbuf_free = [[], []]
                wtil_free = [[], []]
                psS_free = [[], []]
                psO_free = [[], []]

                self.emit("act", lambda e, dummy=dummy: e.activation(
                    dummy[:], self.warmsrc[0:1, 0:NT], AF.Sigmoid),
                    waits=[("gp", 1), ("bar", self.bar_n)], incs=[("act", 1)])
                first = [True]
                for qm in range(NQM):
                    q0 = qm * QM
                    for hp in range(NCD):
                        kpar = hp % 2
                        self.emit("sy", lambda e, h=hp, p=kpar: e.dma_start(
                            kbuf[p][:], self.K_sc[h]),
                            waits=kbuf_free[kpar], incs=[(f"ld_k{kpar}", 16)])
                        k_ld = self.c(f"ld_k{kpar}")
                        if first[0]:
                            first[0] = False

                            def mk_warm(e):
                                for _ in range(32):
                                    inst = e.matmul(psS[1][:, 0:NT], self.warmsrc[:, 0:128],
                                                    self.warmsrc[:, 128:640],
                                                    start=True, stop=True)
                                return inst
                            self.emit("pe", mk_warm,
                                      waits=[("gp", 1), (f"ld_k{kpar}", k_ld)],
                                      incs=[("pe", 1)])
                        for hh in range(2):
                            h = 2 * hp + hh
                            hb = hh * 64
                            opar = h % 2
                            pending_pv = None
                            v_lds = {}
                            for ti, (kc0, nk) in enumerate(tiles):
                                for kc in range(kc0, kc0 + nk):
                                    if kc % 8 == 0:
                                        kb = kc // 8
                                        vpar = kb % 2

                                        def mk_vld(e, hh_=h, kb=kb, p=vpar):
                                            off = hh_ * 128 * NKC_ALL * (DH + 1) + kb * 8 * (DH + 1)
                                            srcp = bass.AP(self.V_sc.tensor, off,
                                                           [[NKC_ALL * (DH + 1), 128],
                                                            [1, 8 * (DH + 1)]])
                                            return e.dma_start(vbuf[p][:], srcp)
                                        self.emit("sy", mk_vld, waits=vbuf_free[vpar],
                                                  incs=[(f"ld_v{vpar}", 16)])
                                        v_lds[kb] = self.c(f"ld_v{vpar}")
                                spar = ti % 2

                                def mk_sc(e, kp=kpar, b=hb, k0=kc0, n=nk, hp_=hp, q0=q0, p=spar):
                                    for j in range(n):
                                        inst = e.matmul(
                                            psS[p][:, j * QM:(j + 1) * QM],
                                            kbuf[kp][b:b + 64, (k0 + j) * 128:(k0 + j + 1) * 128],
                                            self.qbuf[b:b + 64,
                                                      hp_ * T + q0: hp_ * T + q0 + QM],
                                            start=True, stop=True)
                                    return inst
                                self.emit("pe", mk_sc,
                                          waits=[(f"ld_k{kpar}", k_ld)] + psS_free[spar],
                                          incs=[("pe", 1)])
                                sc_done = self.c("pe")
                                self.emit("act", lambda e, p=spar, n=nk: e.activation(
                                    wtil[p][:, 0:n * QM], psS[p][:, 0:n * QM],
                                    AF.Sigmoid, scale=ATT_SCALE),
                                    waits=[("pe", sc_done)] + wtil_free[spar],
                                    incs=[("act", 1)])
                                sig_done = self.c("act")
                                psS_free[spar] = [("act", sig_done)]

                                if pending_pv is not None:
                                    pending_pv()

                                vwaits = []
                                for kc in range(kc0, kc0 + nk):
                                    kb = kc // 8
                                    vwaits.append((f"ld_v{kb % 2}", v_lds[kb]))
                                vwaits = sorted(set(vwaits))

                                def mk_pv_emit(spar=spar, k0=kc0, n=nk, op=opar,
                                               sig=sig_done, vwaits=tuple(vwaits)):
                                    def mk_pv(e):
                                        for j in range(n):
                                            k = k0 + j
                                            a_ = k % 8
                                            vp = (k // 8) % 2
                                            inst = e.matmul(
                                                psO[op][:, :],
                                                vbuf[vp][:, a_ * (DH + 1): a_ * (DH + 1) + DH + 1],
                                                wtil[spar][:, j * QM:(j + 1) * QM],
                                                start=(k == 0), stop=(k == NKC - 1))
                                        return inst
                                    pvw = [("act", sig)] + list(vwaits)
                                    if k0 == 0:
                                        pvw += psO_free[op]
                                    self.emit("pe", mk_pv, waits=pvw, incs=[("pe", 1)])
                                    wtil_free[spar] = [("pe", self.c("pe"))]
                                    for kc in range(k0, k0 + n):
                                        if kc % 8 == 7 or kc == NKC - 1:
                                            vbuf_free[(kc // 8) % 2] = [("pe", self.c("pe"))]
                                pending_pv = mk_pv_emit
                            pending_pv()
                            pending_pv = None
                            pv_all = self.c("pe")
                            rpar = h % 2
                            self.emit("dve", lambda e, op=opar, p=rpar: e.tensor_copy(
                                rs_row[p][:], psO[op][64:65, :]),
                                waits=[("pe", pv_all)] + rs_free[rpar], incs=[("dve", 1)])
                            rsd = self.c("dve")
                            self.emit("sy", lambda e, p=rpar, hh_=h, q=q0: e.dma_start(
                                self.RS_sc[hh_:hh_ + 1, q:q + QM], rs_row[p][:]),
                                waits=[("dve", rsd)], incs=[(f"st_v{rpar}", 16)])
                            rs_free[rpar] = [(f"st_v{rpar}", self.c(f"st_v{rpar}"))]
                            c0 = h // 2
                            self.emit("dve", lambda e, op=opar, b=hb, c=c0, q=q0: e.tensor_copy(
                                self.obuf[b:b + 64, c * T + q: c * T + q + QM], psO[op][0:64, :]),
                                waits=[("pe", pv_all)], incs=[("dve", 1)])
                            psO_free[opar] = [("dve", self.c("dve"))]
                        kbuf_free[kpar] = [("pe", self.c("pe"))]

    def phase2(self):
        nc, T = self.nc, self.T
        NTT = T // NT
        TH = self.TH
        NHF = T // TH

        self.warmup()
        with ExitStack() as sc:
            wob = sc.enter_context(nc.sbuf_tensor("wob", [128, NCD * D], BF16))
            abuf = sc.enter_context(nc.sbuf_tensor("abuf", [128, NCD * T], F32))
            Eb = sc.enter_context(nc.sbuf_tensor("Eb", [16, D], F32))
            RSm = sc.enter_context(nc.sbuf_tensor("RSm", [16, T], F32))
            self.alloc_rows(sc, T, "p2")
            sq2 = [sc.enter_context(nc.sbuf_tensor(f"sq2{i}", [128, T], BF16)) for i in range(2)]
            xb2 = [sc.enter_context(nc.sbuf_tensor(f"xb2{i}", [128, TH], F32)) for i in range(2)]
            y1t = [sc.enter_context(nc.sbuf_tensor(f"y1t{i}", [128, TH], F32)) for i in range(2)]

            self.emit("gp", lambda e: e.dma_start(wob[:], self.wo), incs=[("ld_w1", 16)])
            wo_ready = self.c("ld_w1")

            self.emit("sy", lambda e, RSm=RSm: e.dma_start(RSm[:], self.RS_sc),
                      incs=[("st_rs", 16)])
            self.emit("sy", lambda e, Eb=Eb: e.dma_start(Eb[:], self.Esel),
                      incs=[("st_rs", 16)])
            rsv = self.c("st_rs")
            self.emit("dve", lambda e, RSm=RSm: e.tensor_scalar_add(RSm[:], RSm[:], EPS_ATT),
                      waits=[("st_rs", rsv)], incs=[("dve", 1)])
            self.emit("dve", lambda e, RSm=RSm: e.reciprocal(RSm[:], RSm[:]),
                      waits=[("dve", self.c("dve"))], incs=[("dve", 1)])
            rs_done = self.c("dve")

            with ExitStack() as pctx:
                psB = [pctx.enter_context(nc.psum_tensor(self.u(f"psB{i}"), [128, T], F32)) for i in range(2)]
                psB_free = [[], []]
                for c in range(NCD):
                    par = c % 2

                    def mk_b(e, c=c, p=par, Eb=Eb, RSm=RSm):
                        for tt in range(NTT):
                            inst = e.matmul(psB[p][:, tt * NT:(tt + 1) * NT],
                                            Eb[:, c * 128:(c + 1) * 128],
                                            RSm[0:16, tt * NT:(tt + 1) * NT],
                                            start=True, stop=True)
                        return inst
                    self.emit("pe", mk_b, waits=[("dve", rs_done)] + psB_free[par],
                              incs=[("pe", 1)])
                    bv = self.c("pe")
                    self.emit("dve", lambda e, c=c, p=par: e.tensor_mul(
                        self.obuf[:, c * T:(c + 1) * T], self.obuf[:, c * T:(c + 1) * T], psB[p][:]),
                        waits=[("pe", bv)], incs=[("dve", 1)])
                    psB_free[par] = [("dve", self.c("dve"))]
                onorm_done = self.c("dve")

            with ExitStack() as pctx:
                psA = [pctx.enter_context(nc.psum_tensor(self.u(f"psA{i}"), [128, T], F32)) for i in range(2)]
                psA_free = [[], []]
                for oc in range(NCD):
                    par = oc % 2

                    def mk_wo(e, o=oc, p=par):
                        for ic in range(NCD):
                            for tt in range(NTT):
                                inst = e.matmul(
                                    psA[p][:, tt * NT:(tt + 1) * NT],
                                    wob[:, ic * D + o * 128: ic * D + o * 128 + 128],
                                    self.obuf[:, ic * T + tt * NT: ic * T + (tt + 1) * NT],
                                    start=(ic == 0), stop=(ic == NCD - 1))
                        return inst
                    self.emit("pe", mk_wo,
                              waits=[("dve", onorm_done), ("ld_w1", wo_ready)] + psA_free[par],
                              incs=[("pe", 1)])
                    mmv = self.c("pe")
                    self.emit("dve", lambda e, o=oc, p=par: e.tensor_copy(
                        abuf[:, o * T:(o + 1) * T], psA[p][:]),
                        waits=[("pe", mmv)], incs=[("dve", 1)])
                    psA_free[par] = [("dve", self.c("dve"))]
                a_done = self.c("dve")

            with ExitStack() as pctx:
                ps_sum = pctx.enter_context(nc.psum_tensor(self.u("ps_sum2"), [1, T], F32))
                ps_ssq = pctx.enter_context(nc.psum_tensor(self.u("ps_ssq2"), [1, T], F32))
                sq_free = [[], []]
                for ic in range(NCD):
                    par = ic % 2
                    self.emit("act", lambda e, i=ic, p=par: e.activation(
                        sq2[p][:], abuf[:, i * T:(i + 1) * T], AF.Square),
                        waits=[("dve", a_done)] + sq_free[par], incs=[("act", 1)])
                    actv = self.c("act")

                    def mk_stats(e, i=ic, p=par):
                        for tt in range(NTT):
                            e.matmul(ps_sum[0:1, tt * NT:(tt + 1) * NT], self.onesF[:],
                                     abuf[:, i * T + tt * NT: i * T + (tt + 1) * NT],
                                     start=(i == 0), stop=(i == NCD - 1))
                        for tt in range(NTT):
                            inst = e.matmul(ps_ssq[0:1, tt * NT:(tt + 1) * NT], self.onesB[:],
                                            sq2[p][:, tt * NT:(tt + 1) * NT],
                                            start=(i == 0), stop=(i == NCD - 1))
                        return inst
                    self.emit("pe", mk_stats, waits=[("act", actv), ("dve", a_done)],
                              incs=[("pe", 1)])
                    sq_free[par] = [("pe", self.c("pe"))]
                rows_done = self.ln_rows(ps_sum, ps_ssq, T, self.rows["r"], self.rows["mr"])

            xb_free = [[], []]
            y1_free = [[], []]
            for hf in range(NHF):
                t0 = hf * TH
                NTT_H = TH // NT
                with ExitStack() as pctx:
                    psR = pctx.enter_context(nc.psum_tensor(self.u("psRy"), [128, TH], F32))
                    psMR = pctx.enter_context(nc.psum_tensor(self.u("psMRy"), [128, TH], F32))
                    ps2s = pctx.enter_context(nc.psum_tensor(self.u("ps2s"), [1, TH], F32))
                    ps2q = pctx.enter_context(nc.psum_tensor(self.u("ps2q"), [1, TH], F32))
                    bc_done = self.bcast(psR, psMR, self.rows["r"], self.rows["mr"], t0, NTT_H,
                                         waits=[("dve", rows_done)])
                    for ic in range(NCD):
                        par = ic % 2
                        self.emit("sy", lambda e, i=ic, p=par, t0=t0: e.dma_start(
                            xb2[p][:], self.xT[i * 128:(i + 1) * 128, t0:t0 + TH]),
                            waits=xb_free[par], incs=[(f"ld_x{par}", 16)])
                        ldv = self.c(f"ld_x{par}")
                        self.emit("dve", lambda e, i=ic, t0=t0, psR=psR: e.tensor_mul(
                            abuf[:, i * T + t0: i * T + t0 + TH],
                            abuf[:, i * T + t0: i * T + t0 + TH], psR[:]),
                            waits=[("pe", bc_done)], incs=[("dve", 1)])
                        self.emit("dve", lambda e, i=ic, t0=t0, psMR=psMR: e.tensor_sub(
                            abuf[:, i * T + t0: i * T + t0 + TH],
                            abuf[:, i * T + t0: i * T + t0 + TH], psMR[:]),
                            waits=[("dve", self.c("dve"))], incs=[("dve", 1)])
                        self.emit("dve", lambda e, i=ic, p=par, t0=t0: e.tensor_add(
                            y1t[p][:], xb2[p][:], abuf[:, i * T + t0: i * T + t0 + TH]),
                            waits=[("dve", self.c("dve")), (f"ld_x{par}", ldv)] + y1_free[par],
                            incs=[("dve", 1)])
                        dvv = self.c("dve")
                        xb_free[par] = [("dve", dvv)]
                        self.emit("sy", lambda e, i=ic, p=par, t0=t0: e.dma_start(
                            self.y1_sc[i, :, t0:t0 + TH], y1t[p][:]),
                            waits=[("dve", dvv)], incs=[(f"st_y{par}", 16)])
                        self.emit("act", lambda e, p=par: e.activation(
                            sq2[p][:, 0:TH], y1t[p][:], AF.Square),
                            waits=[("dve", dvv)], incs=[("act", 1)])
                        actv = self.c("act")

                        def mk_stats2(e, i=ic, p=par, ps2s=ps2s, ps2q=ps2q):
                            for tt in range(NTT_H):
                                e.matmul(ps2s[0:1, tt * NT:(tt + 1) * NT], self.onesF[:],
                                         y1t[p][:, tt * NT:(tt + 1) * NT],
                                         start=(i == 0), stop=(i == NCD - 1))
                            for tt in range(NTT_H):
                                inst = e.matmul(ps2q[0:1, tt * NT:(tt + 1) * NT], self.onesB[:],
                                                sq2[p][:, tt * NT:(tt + 1) * NT],
                                                start=(i == 0), stop=(i == NCD - 1))
                            return inst
                        gw2 = self.guard() if ic == 0 else []
                        self.emit("pe", mk_stats2, waits=[("act", actv)] + gw2, incs=[("pe", 1)])
                        y1_free[par] = [("pe", self.c("pe")),
                                        (f"st_y{par}", self.c(f"st_y{par}"))]
                    self.ln_rows(ps2s, ps2q, TH, self.r2row, self.mr2row, roff=t0)

    def phase3(self):
        nc, T, TH = self.nc, self.T, self.TH
        NHF = T // TH
        NTT = TH // NT

        self.warmup()
        with ExitStack() as sc:
            fb1 = sc.enter_context(nc.sbuf_tensor("fb1", [128, NCH * TH], BF16))
            wfb = [sc.enter_context(nc.sbuf_tensor(f"wfb{i}", [128, NCH * 128], BF16))
                   for i in range(2)]
            y1t = [sc.enter_context(nc.sbuf_tensor(f"y1f{i}", [128, TH], F32)) for i in range(2)]
            outb = [sc.enter_context(nc.sbuf_tensor(f"outb{i}", [128, TH], F32)) for i in range(2)]
            sq3 = [sc.enter_context(nc.sbuf_tensor(f"sq3{i}", [128, TH], BF16)) for i in range(2)]
            self.alloc_rows(sc, TH, "p3")

            wfb_free = [[], []]
            y1_free = [[], []]
            out_free = [[], []]
            wk = [0]

            def stream_w(src_ap):
                k = wk[0]
                b = k % 2
                self.emit("gp", lambda e, s=src_ap, b=b: e.dma_start(wfb[b][:, 0:s.shape[-1]], s),
                          waits=wfb_free[b], incs=[(f"ld_w{b}", 16)])
                wk[0] += 1
                return b, self.c(f"ld_w{b}")

            for hf in range(NHF):
                t0 = hf * TH

                with ExitStack() as lnctx:
                    ln2t = lnctx.enter_context(nc.sbuf_tensor(self.u("ln2t"), [128, NCD * TH], BF16))
                    with ExitStack() as pctx:
                        psR = pctx.enter_context(nc.psum_tensor(self.u("psR3"), [128, TH], F32))
                        psMR = pctx.enter_context(nc.psum_tensor(self.u("psMR3"), [128, TH], F32))
                        bc_done = self.bcast(psR, psMR, self.r2row, self.mr2row, t0, NTT,
                                             waits=[("dve", self.c("dve"))])
                        for ic in range(NCD):
                            par = ic % 2
                            self.emit("sy", lambda e, i=ic, p=par, t0=t0: e.dma_start(
                                y1t[p][:], self.y1_sc[i, :, t0:t0 + TH]),
                                waits=y1_free[par], incs=[(f"ld_y{par}", 16)])
                            ldv = self.c(f"ld_y{par}")
                            self.emit("dve", lambda e, p=par, psR=psR: e.tensor_mul(
                                y1t[p][:], y1t[p][:], psR[:]),
                                waits=[(f"ld_y{par}", ldv), ("pe", bc_done)], incs=[("dve", 1)])
                            self.emit("dve", lambda e, i=ic, p=par, psMR=psMR, ln2t=ln2t: e.tensor_sub(
                                ln2t[:, i * TH:(i + 1) * TH], y1t[p][:], psMR[:]),
                                waits=[("dve", self.c("dve"))], incs=[("dve", 1)])
                            y1_free[par] = [("dve", self.c("dve"))]
                        ln2_done = self.c("dve")

                    with ExitStack() as pctx:
                        psF = [pctx.enter_context(nc.psum_tensor(self.u(f"psF{i}"), [128, TH], F32))
                               for i in range(2)]
                        psF_free = [[], []]
                        for stage_i, (wsrc, scl) in enumerate([(self.w1, self.s1b),
                                                               (self.w2, self.s2b)]):
                            for oc in range(NCH):
                                b, wld = stream_w(wsrc[oc])
                                par = oc % 2

                                def mk_mm(e, b=b, p=par, psF=psF, ln2t=ln2t):
                                    for ic in range(NCD):
                                        for tt in range(NTT):
                                            inst = e.matmul(
                                                psF[p][:, tt * NT:(tt + 1) * NT],
                                                wfb[b][:, ic * 128:(ic + 1) * 128],
                                                ln2t[:, ic * TH + tt * NT: ic * TH + (tt + 1) * NT],
                                                start=(ic == 0), stop=(ic == NCD - 1))
                                    return inst
                                self.emit("pe", mk_mm,
                                          waits=[("dve", ln2_done), (f"ld_w{b}", wld)] + psF_free[par],
                                          incs=[("pe", 1)])
                                mmv = self.c("pe")
                                wfb_free[b] = [("pe", mmv)]
                                if stage_i == 0:
                                    self.emit("act", lambda e, o=oc, p=par, psF=psF, scl=scl: e.activation(
                                        fb1[:, o * TH:(o + 1) * TH], psF[p][:], AF.Silu,
                                        scale=scl[:, o:o + 1]),
                                        waits=[("pe", mmv)], incs=[("act", 1)])
                                    psF_free[par] = [("act", self.c("act"))]
                                else:
                                    self.emit("dve", lambda e, o=oc, p=par, psF=psF, scl=scl: e.scalar_tensor_tensor(
                                        fb1[:, o * TH:(o + 1) * TH], psF[p][:], scl[:, o:o + 1],
                                        fb1[:, o * TH:(o + 1) * TH], ALU.mult, ALU.mult),
                                        waits=[("pe", mmv), ("act", self.c("act"))],
                                        incs=[("dve", 1)])
                                    psF_free[par] = [("dve", self.c("dve"))]
                        h_in = fb1

                with ExitStack() as pctx:
                    fb2 = pctx.enter_context(nc.sbuf_tensor(self.u("fb2"), [128, NCH * TH], BF16))
                    psF = [pctx.enter_context(nc.psum_tensor(self.u(f"psG{i}"), [128, TH], F32))
                           for i in range(2)]
                    psF_free = [[], []]
                    for stage_i, (wsrc, scl, src, dst) in enumerate([
                            (self.h1, self.hs1b, fb1, fb2),
                            (self.h2, self.hs2b, fb2, fb1)]):
                        src_ready = [("dve", self.c("dve")), ("act", self.c("act"))]
                        for oc in range(NCH):
                            b, wld = stream_w(wsrc[oc])
                            par = oc % 2

                            def mk_mm(e, b=b, p=par, s=src, psF=psF):
                                for ic in range(NCH):
                                    for tt in range(NTT):
                                        inst = e.matmul(
                                            psF[p][:, tt * NT:(tt + 1) * NT],
                                            wfb[b][:, ic * 128:(ic + 1) * 128],
                                            s[:, ic * TH + tt * NT: ic * TH + (tt + 1) * NT],
                                            start=(ic == 0), stop=(ic == NCH - 1))
                                return inst
                            self.emit("pe", mk_mm,
                                      waits=src_ready + [(f"ld_w{b}", wld)] + psF_free[par],
                                      incs=[("pe", 1)])
                            mmv = self.c("pe")
                            wfb_free[b] = [("pe", mmv)]
                            self.emit("act", lambda e, o=oc, p=par, d=dst, sc_=scl, psF=psF: e.activation(
                                d[:, o * TH:(o + 1) * TH], psF[p][:], AF.Gelu,
                                scale=sc_[:, o:o + 1]),
                                waits=[("pe", mmv)], incs=[("act", 1)])
                            psF_free[par] = [("act", self.c("act"))]

                with ExitStack() as fctx:
                    fbf = fctx.enter_context(nc.sbuf_tensor(self.u("fbf"), [128, NCD * TH], BF16))
                    with ExitStack() as pctx:
                        psF = [pctx.enter_context(nc.psum_tensor(self.u(f"psW{i}"), [128, TH], F32))
                               for i in range(2)]
                        ps3s = pctx.enter_context(nc.psum_tensor(self.u("ps3s"), [1, TH], F32))
                        ps3q = pctx.enter_context(nc.psum_tensor(self.u("ps3q"), [1, TH], F32))
                        psF_free = [[], []]
                        src_ready = [("act", self.c("act"))]
                        sq_free = [[], []]
                        for oc in range(NCD):
                            b, wld = stream_w(self.w3[oc])
                            par = oc % 2

                            def mk_mm(e, b=b, p=par, psF=psF):
                                for ic in range(NCH):
                                    for tt in range(NTT):
                                        inst = e.matmul(
                                            psF[p][:, tt * NT:(tt + 1) * NT],
                                            wfb[b][:, ic * 128:(ic + 1) * 128],
                                            fb1[:, ic * TH + tt * NT: ic * TH + (tt + 1) * NT],
                                            start=(ic == 0), stop=(ic == NCH - 1))
                                return inst
                            self.emit("pe", mk_mm,
                                      waits=src_ready + [(f"ld_w{b}", wld)] + psF_free[par],
                                      incs=[("pe", 1)])
                            mmv = self.c("pe")
                            wfb_free[b] = [("pe", mmv)]
                            self.emit("dve", lambda e, o=oc, p=par, psF=psF, fbf=fbf: e.tensor_scalar_mul(
                                fbf[:, o * TH:(o + 1) * TH], psF[p][:], self.s3b[:, o:o + 1]),
                                waits=[("pe", mmv)], incs=[("dve", 1)])
                            psF_free[par] = [("dve", self.c("dve"))]
                            fv = self.c("dve")
                            self.emit("act", lambda e, o=oc, p=par, fbf=fbf: e.activation(
                                sq3[p][:], fbf[:, o * TH:(o + 1) * TH], AF.Square),
                                waits=[("dve", fv)] + sq_free[par], incs=[("act", 1)])
                            actv = self.c("act")

                            def mk_stats(e, o=oc, p=par, ps3s=ps3s, ps3q=ps3q, fbf=fbf):
                                for tt in range(NTT):
                                    e.matmul(ps3s[0:1, tt * NT:(tt + 1) * NT], self.onesB[:],
                                             fbf[:, o * TH + tt * NT: o * TH + (tt + 1) * NT],
                                             start=(o == 0), stop=(o == NCD - 1))
                                for tt in range(NTT):
                                    inst = e.matmul(ps3q[0:1, tt * NT:(tt + 1) * NT], self.onesB[:],
                                                    sq3[p][:, tt * NT:(tt + 1) * NT],
                                                    start=(o == 0), stop=(o == NCD - 1))
                                return inst
                            self.emit("pe", mk_stats, waits=[("act", actv), ("dve", fv)],
                                      incs=[("pe", 1)])
                            sq_free[par] = [("pe", self.c("pe"))]
                        rows_done = self.ln_rows(ps3s, ps3q, TH, self.rows["r"], self.rows["mr"])

                    with ExitStack() as pctx:
                        psR = pctx.enter_context(nc.psum_tensor(self.u("psRf"), [128, TH], F32))
                        psMR = pctx.enter_context(nc.psum_tensor(self.u("psMRf"), [128, TH], F32))
                        bc_done = self.bcast(psR, psMR, self.rows["r"], self.rows["mr"], 0, NTT,
                                             waits=[("dve", rows_done)])
                        for ic in range(NCD):
                            par = ic % 2
                            self.emit("sy", lambda e, i=ic, p=par, t0=t0: e.dma_start(
                                y1t[p][:], self.y1_sc[i, :, t0:t0 + TH]),
                                waits=y1_free[par], incs=[(f"ld_y{par}", 16)])
                            ldv = self.c(f"ld_y{par}")
                            self.emit("dve", lambda e, i=ic, p=par, psR=psR, fbf=fbf: e.tensor_mul(
                                outb[p][:], fbf[:, i * TH:(i + 1) * TH], psR[:]),
                                waits=[("pe", bc_done)] + out_free[par], incs=[("dve", 1)])
                            self.emit("dve", lambda e, p=par, psMR=psMR: e.tensor_sub(
                                outb[p][:], outb[p][:], psMR[:]),
                                waits=[("dve", self.c("dve"))], incs=[("dve", 1)])
                            self.emit("dve", lambda e, p=par: e.tensor_add(
                                outb[p][:], outb[p][:], y1t[p][:]),
                                waits=[("dve", self.c("dve")), (f"ld_y{par}", ldv)],
                                incs=[("dve", 1)])
                            dvv = self.c("dve")
                            y1_free[par] = [("dve", dvv)]
                            self.emit("sy", lambda e, i=ic, p=par, t0=t0: e.dma_start(
                                self.out[i * 128:(i + 1) * 128, t0:t0 + TH], outb[p][:]),
                                waits=[("dve", dvv)], incs=[(f"st_o{par}", 16)])
                            out_free[par] = [(f"st_o{par}", self.c(f"st_o{par}"))]

    def debug_p0(self):
        nc = self.nc
        dQ = nc.dram_tensor("dQ", [128, NCD * self.T], BF16, kind="ExternalOutput").ap()
        dK = nc.dram_tensor("dK", [NCD, 128, self.S], BF16, kind="ExternalOutput").ap()
        dV = nc.dram_tensor("dV", [NH, self.S, DH], BF16, kind="ExternalOutput").ap()
        self.emit("sy", lambda e: e.dma_start(dQ, self.qbuf[:]), incs=[("st_o0", 16)])
        for oc in range(NCD):
            self.emit("sy", lambda e, o=oc: e.dma_start(dK[o], self.K_sc[o]), incs=[("st_o0", 16)])
        for h in range(NH):
            self.emit("sy", lambda e, h=h: e.dma_start(dV[h], self.V_sc[h]), incs=[("st_o0", 16)])

    def debug_p1(self):
        nc = self.nc
        dO = nc.dram_tensor("dO", [128, NCD * self.T], BF16, kind="ExternalOutput").ap()
        dRS = nc.dram_tensor("dRS", [16, self.T], F32, kind="ExternalOutput").ap()
        self.emit("sy", lambda e: e.dma_start(dO, self.obuf[:]), incs=[("st_o0", 16)])
        self.emit("sy", lambda e: e.dma_start(dRS, self.RS_sc), incs=[("st_o0", 16)])

    def debug_p2(self):
        nc = self.nc
        dY1 = nc.dram_tensor("dY1", [NCD, 128, self.T], F32, kind="ExternalOutput").ap()
        dR2 = nc.dram_tensor("dR2", [1, self.T], F32, kind="ExternalOutput").ap()
        dMR2 = nc.dram_tensor("dMR2", [1, self.T], F32, kind="ExternalOutput").ap()
        for ic in range(NCD):
            self.emit("sy", lambda e, i=ic: e.dma_start(dY1[i], self.y1_sc[i]), incs=[("st_o0", 16)])
        self.emit("sy", lambda e: e.dma_start(dR2, self.r2row[:]), incs=[("st_o0", 16)])
        self.emit("sy", lambda e: e.dma_start(dMR2, self.mr2row[:]), incs=[("st_o0", 16)])

    def finish(self):
        waits = [(s, self.cnt[s]) for s in self.cnt if self.cnt[s] > 0 and s != "bar"]
        self.emit("sy", None, waits=waits)

    def realize(self):
        with self.nc.Block() as blk:
            @blk.tensor
            def _(t):
                for fn in self.streams["pe"]:
                    fn(t)

            @blk.scalar
            def _(s):
                for fn in self.streams["act"]:
                    fn(s)

            @blk.vector
            def _(v):
                for fn in self.streams["dve"]:
                    fn(v)

            @blk.sync
            def _(sy):
                for fn in self.streams["sy"]:
                    fn(sy)

            @blk.gpsimd
            def _(g):
                for fn in self.streams["gp"]:
                    fn(g)
        self.ctx.close()
        return self.nc


def build(S=8192, stage="full"):
    b = Builder(S, stage)
    b.build()
    return b.realize()


def prep_weights(inputs):
    def blk_proj(W):
        Wt = np.ascontiguousarray(np.asarray(W, np.float32).T)
        return np.ascontiguousarray(
            Wt.reshape(NCD, 128, D).transpose(1, 0, 2).reshape(128, NCD * D))

    def blk_ffn(W, nic, noc):
        Wt = np.ascontiguousarray(np.asarray(W, np.float32).T)
        return np.ascontiguousarray(
            Wt.reshape(nic, 128, noc, 128).transpose(2, 1, 0, 3).reshape(noc, 128, nic * 128))

    E = np.zeros((16, D), np.float32)
    for h in range(NH):
        E[h, h * DH:(h + 1) * DH] = 1.0
    f32 = lambda a: np.asarray(a, np.float32)
    return {
        "wq": blk_proj(inputs["L0_attn_Wq"]),
        "wk": blk_proj(inputs["L0_attn_Wk"]),
        "wv": blk_proj(inputs["L0_attn_Wv"]),
        "wo": blk_proj(inputs["L0_attn_Wo"]),
        "w1": blk_ffn(inputs["L0_ffn_w1_w"], NCD, NCH),
        "w2": blk_ffn(inputs["L0_ffn_w2_w"], NCD, NCH),
        "h1": blk_ffn(inputs["L0_ffn_h1_w"], NCH, NCH),
        "h2": blk_ffn(inputs["L0_ffn_h2_w"], NCH, NCH),
        "w3": blk_ffn(inputs["L0_ffn_w3_w"], NCH, NCD),
        "s1": np.ascontiguousarray(f32(inputs["L0_ffn_w1_s"]).reshape(NCH, 128).T),
        "s2": np.ascontiguousarray(f32(inputs["L0_ffn_w2_s"]).reshape(NCH, 128).T),
        "hs1": np.ascontiguousarray(f32(inputs["L0_ffn_h1_s"]).reshape(NCH, 128).T),
        "hs2": np.ascontiguousarray(f32(inputs["L0_ffn_h2_s"]).reshape(NCH, 128).T),
        "s3": np.ascontiguousarray(f32(inputs["L0_ffn_w3_s"]).reshape(NCD, 128).T),
        "Esel": E,
    }


def make_in_maps(inputs, S):
    T = S // 4
    w = prep_weights(inputs)
    x = np.asarray(inputs["L0_attn_x"], np.float32)
    maps = []
    for c in range(8):
        b, q = c // 4, c % 4
        xT = x[b].T
        rot = np.concatenate([xT[:, q * T:(q + 1) * T], xT[:, :q * T], xT[:, (q + 1) * T:]], axis=1)
        m = {"xT": np.ascontiguousarray(rot)}
        m.update(w)
        maps.append(m)
    return maps


def kernel(**inputs):
    x = np.asarray(inputs["L0_attn_x"], np.float32)
    B, S, _ = x.shape
    T = S // 4
    nc = build(S=S, stage="full")
    maps = make_in_maps(inputs, S)
    res = run_bass_kernel_spmd(nc, maps, core_ids=list(range(8)))
    out = np.empty((B, S, D), np.float32)
    for c in range(8):
        b, q = c // 4, c % 4
        out[b, q * T:(q + 1) * T, :] = np.asarray(res.results[c]["out"], np.float32).T
    return out
